# version 04
# baseline (speedup 1.0000x reference)
import math
import contextlib
import numpy as np
import ml_dtypes
import concourse.bass as bass
import concourse.mybir as mybir
from concourse.bass_utils import run_bass_kernel_spmd

F32 = mybir.dt.float32
BF16 = mybir.dt.bfloat16
AF = mybir.ActivationFunctionType
ALU = mybir.AluOpType
AX = mybir.AxisListType
bfdt = ml_dtypes.bfloat16

NCORES = 8
TOK = 1024
DMA_WIN = 8
TEXT = TOK + 3
NCH = 8
LN_EPS = 1e-5
EPS6 = 1e-6
FFN_SCALE = 0.5 * math.sqrt(2.0)
BIG = 1e30
QSCALE_BIAS = -0.5 * math.log(128.0)


def _hadamard(n):
    H = np.array([[1.0]], dtype=np.float64)
    while H.shape[0] < n:
        H = np.block([[H, H], [H, -H]])
    return H


def host_consts():
    c = {}
    c["H128"] = _hadamard(128).astype(bfdt)
    H64 = _hadamard(64)
    M2 = np.zeros((128, 128), np.float64)
    for po in range(128):
        for pi in range(128):
            if (po & 1) == (pi & 1):
                M2[po, pi] = H64[po >> 1, pi >> 1]
    c["M2"] = (M2 * 2.0 ** -7).astype(bfdt)
    c["I128B"] = np.eye(128, dtype=bfdt)
    c["I128F"] = np.eye(128, dtype=np.float32)
    c["ONES1B"] = np.ones((1, 128), bfdt)
    c["ONES1F"] = np.ones((1, 128), np.float32)
    c["ONESCB"] = np.ones((128, 1), bfdt)
    i = np.arange(128)
    c["UPONES"] = (i[:, None] <= i[None, :]).astype(bfdt)
    c["UPONESF"] = (i[:, None] <= i[None, :]).astype(np.float32)
    c["LMASK_S"] = np.where(i[:, None] > i[None, :], 0.0, BIG).astype(np.float32)
    c["LMASK_IT"] = np.where(i[None, :] >= i[:, None], 0.0, -BIG).astype(np.float32)
    fc = np.zeros((128, 4), np.float32)
    fc[:, 0] = LN_EPS
    fc[:, 1] = EPS6
    fc[:, 2] = 1.0
    fc[:, 3] = QSCALE_BIAS
    c["FC"] = fc
    return c


CONST_SPECS = [("H128", [128, 128], BF16), ("M2", [128, 128], BF16),
               ("I128B", [128, 128], BF16), ("I128F", [128, 128], F32),
               ("ONES1B", [1, 128], BF16), ("ONES1F", [1, 128], F32),
               ("ONESCB", [128, 1], BF16), ("UPONES", [128, 128], BF16),
               ("LMASK_S", [128, 128], F32), ("LMASK_IT", [128, 128], F32),
               ("FC", [128, 4], F32), ("UPONESF", [128, 128], F32)]


class Prog:

    def __init__(self, nc):
        self.nc = nc
        self.ops = {k: [] for k in "TVSYG"}
        self.marked = set()
        self._es = contextlib.ExitStack()
        self.sems = {k: self._es.enter_context(nc.semaphore("sem" + k))
                     for k in "TVS"}
        self.dsems = {k: [self._es.enter_context(nc.semaphore("sem%s%d" % (k, j)))
                          for j in range(DMA_WIN)] for k in "YG"}

    def op(self, eng, fn, deps=(), dma=False):
        idx = len(self.ops[eng])
        dd = []
        for d in deps:
            if d is None:
                continue
            dd.append(d)
            self.marked.add(d)
        self.ops[eng].append((fn, dd, dma))
        return (eng, idx)

    def all_events(self):
        ev = [(k, len(self.ops[k]) - 1) for k in "TVS" if self.ops[k]]
        for k in "YG":
            dix = [i for i, o in enumerate(self.ops[k]) if o[2]]
            ev += [(k, i) for i in dix[-DMA_WIN:]]
        return ev

    def realize(self):
        val = {}
        for k in "TVS":
            v = [0] * (len(self.ops[k]) + 1)
            c = 0
            for i, (fn, deps, dma) in enumerate(self.ops[k]):
                if (k, i) in self.marked:
                    c += 1
                v[i + 1] = c
            val[k] = v
        orda = {}
        for k in "YG":
            o = []
            d = 0
            for (fn, deps, dma) in self.ops[k]:
                o.append(d if dma else None)
                d += 1 if dma else 0
            orda[k] = o
        prog = self

        def wait_dma(eng, seen, de, di):
            d = orda[de][di]
            s = d % DMA_WIN
            need = 16 * (d // DMA_WIN + 1)
            if need > seen.get((de, s), 0):
                eng.wait_ge(prog.dsems[de][s], need)
                seen[(de, s)] = need

        def body(eng, k):
            seen = {}
            for i, (fn, deps, dma) in enumerate(prog.ops[k]):
                for (de, di) in deps:
                    if de in "YG":
                        if orda[de][di] is None:
                            continue
                        wait_dma(eng, seen, de, di)
                        continue
                    if de == k and (de, di) not in prog.marked:
                        continue
                    need = val[de][di + 1]
                    if need > seen.get(de, 0):
                        eng.wait_ge(prog.sems[de], need)
                        seen[de] = need
                if dma and orda[k][i] >= DMA_WIN:
                    d = orda[k][i] - DMA_WIN
                    s = d % DMA_WIN
                    need = 16 * (d // DMA_WIN + 1)
                    if need > seen.get((k, s), 0):
                        eng.wait_ge(prog.dsems[k][s], need)
                        seen[(k, s)] = need
                ins = fn(eng)
                if dma:
                    ins.then_inc(prog.dsems[k][orda[k][i] % DMA_WIN], 16)
                elif (k, i) in prog.marked:
                    ins.then_inc(prog.sems[k], 1)

        blk = self._es.enter_context(self.nc.Block())
        blk.tensor(lambda e: body(e, "T"))
        blk.vector(lambda e: body(e, "V"))
        blk.scalar(lambda e: body(e, "S"))
        blk.sync(lambda e: body(e, "Y"))
        blk.gpsimd(lambda e: body(e, "G"))
        self._es.close()


class Bank:

    def __init__(self, t):
        self.t = t
        self.t16 = t.bitcast(BF16)
        self.prod = None
        self._cons = {}

    @property
    def cons(self):
        return list(self._cons.values())


class B:

    def __init__(self, nc, P, es):
        self.nc, self.P, self.es = nc, P, es
        self.sb = {}
        self.banks = {}

    def sbuf(self, name, shape, dtype):
        t = self.es.enter_context(self.nc.sbuf_tensor(name, shape, dtype))
        self.sb[name] = t
        return t

    def mkbank(self, name):
        t = self.es.enter_context(self.nc.psum_tensor(name, [128, 512], F32))
        bk = Bank(t)
        self.banks[name] = bk
        return bk

    def dma(self, q, dst, src, deps=()):
        return self.P.op(q, lambda e: e.dma_start(dst, src), deps, dma=True)

    def mm(self, bank, sl, lhsT, rhs, start, stop, deps=()):
        d = list(deps)
        if start:
            d.extend(bank.cons)
        else:
            d.append(bank.prod)
        ev = self.P.op("T", lambda e: e.matmul(bank.t[sl], lhsT, rhs,
                                               start=start, stop=stop), d)
        bank.prod = ev
        return ev

    def trans(self, bank, sl, in_, ident, deps=(), f32=False):
        d = list(deps) + bank.cons
        t = bank.t if f32 else bank.t16
        ev = self.P.op("T", lambda e: e.transpose(t[sl], in_, ident), d)
        bank.prod = ev
        return ev

    def rd(self, bank, ev):
        bank._cons[ev[0]] = ev
        return ev

    def v(self, fn, deps=()):
        return self.P.op("V", fn, deps)

    def s(self, fn, deps=()):
        return self.P.op("S", fn, deps)

    def copy(self, out, in_, deps=()):
        return self.P.op("V", lambda e: e.tensor_copy(out, in_), deps)

    def scopy(self, out, in_, deps=()):
        return self.P.op("S", lambda e: e.activation(out, in_, AF.Copy), deps)

    def tt(self, out, a, bb, op, deps=()):
        return self.P.op("V", lambda e: e.tensor_tensor(out, a, bb, op), deps)

    def ts(self, out, in0, s1, s2, op0, op1, deps=()):
        return self.P.op("V", lambda e: e.tensor_scalar(out, in0, s1, s2, op0, op1), deps)

    def ts1(self, out, in0, s1, op, deps=()):
        return self.P.op("V", lambda e: e.tensor_single_scalar(out, in0, s1, op), deps)

    def stt(self, out, in0, scal, in1, op0, op1, deps=()):
        return self.P.op("V", lambda e: e.scalar_tensor_tensor(out, in0, scal, in1, op0, op1), deps)

    def act(self, out, in_, func, bias=0.0, scale=1.0, deps=()):
        return self.P.op("S", lambda e: e.activation(out, in_, func, bias=bias, scale=scale), deps)

    def memset(self, ap, val, deps=()):
        return self.P.op("V", lambda e: e.memset(ap, val), deps)

    def reduce(self, out, in_, op, deps=()):
        return self.P.op("V", lambda e: e.tensor_reduce(out, in_, AX.X, op), deps)

    def recip(self, out, in_, deps=()):
        return self.P.op("V", lambda e: e.reciprocal(out, in_), deps)


def emit_ln(b, x_buf, nt, mu_bc, rstd_bc, dep_x_list, bank, bank2, pfx):
    xb = b.sb[pfx + "xb"]
    sq = b.sb[pfx + "sq"]
    st = b.sb[pfx + "st"]
    rows = b.sb[pfx + "rb"]
    nblk = (nt + 511) // 512
    mm_s = mm_q = None
    ev_blk = []
    for blk in range(nblk):
        c0 = blk * 512
        cw = min(512, nt - c0)
        for kt in range(16):
            xs = x_buf[:, kt * nt + c0: kt * nt + c0 + cw]
            ec = b.copy(xb[:, 0:cw], xs, deps=list(dep_x_list) + [mm_s, mm_q])
            eq = b.act(sq[:, 0:cw], xb[:, 0:cw], AF.Square, deps=[ec])
            mm_s = b.mm(bank, (slice(0, 1), slice(0, cw)), b.sb["ONESCB"][:, :],
                        xb[:, 0:cw], start=(kt == 0), stop=(kt == 15), deps=[ec])
            mm_q = b.mm(bank2, (slice(0, 1), slice(0, cw)), b.sb["ONESCB"][:, :],
                        sq[:, 0:cw], start=(kt == 0), stop=(kt == 15), deps=[eq])
        st = b.sb[pfx + "st"]
        e1 = b.rd(bank, b.ts1(st[:, c0:c0 + cw], bank.t[0:1, 0:cw], 1.0 / 2048,
                              ALU.mult, deps=[mm_s]))
        e2 = b.rd(bank2, b.ts1(st[:, 1030 + c0:1030 + c0 + cw], bank2.t[0:1, 0:cw],
                               1.0 / 2048, ALU.mult, deps=[mm_q]))
        ev_blk += [e1, e2]
        mm_s, mm_q = e1, e2
    st = b.sb[pfx + "st"]
    mu = st[:, 0:nt]
    var = st[:, 1030:1030 + nt]
    rst = st[:, 2060:2060 + nt]
    e3 = b.tt(rst, mu, mu, ALU.mult, deps=ev_blk)
    e4 = b.tt(var, var, rst, ALU.subtract, deps=[e3])
    e5 = b.act(rst, var, AF.Ln, bias=b.sb["FC"][0:1, 0:1], scale=1.0, deps=[e4])
    e6 = b.act(rst, rst, AF.Exp, bias=0.0, scale=-0.5, deps=[e5])
    e1 = ev_blk[-2]
    e7 = b.copy(rows[:, 0:nt], mu, deps=[e1])
    e8 = b.copy(rows[:, 1030:1030 + nt], rst, deps=[e6])
    last = None
    for blk in range(nblk):
        c0 = blk * 512
        cw = min(512, nt - c0)
        em = b.mm(bank, (slice(0, 128), slice(0, cw)), b.sb["ONES1B"][:, :],
                  rows[:, c0:c0 + cw], True, True, deps=[e7])
        ec = b.rd(bank, b.copy(mu_bc[:, c0:c0 + cw], bank.t[:, 0:cw], deps=[em]))
        er = b.mm(bank2, (slice(0, 128), slice(0, cw)), b.sb["ONES1B"][:, :],
                  rows[:, 1030 + c0:1030 + c0 + cw], True, True, deps=[e8])
        last = b.rd(bank2, b.copy(rstd_bc[:, c0:c0 + cw], bank2.t[:, 0:cw],
                                  deps=[er, ec]))
    return last


DEBUG_APS = {}


def emit_ffn(b, x_buf, nt, qs, qoff, w1t, w3t, w2t, s1buf, fwbuf, pfx, dep_start):
    P = b.P
    bkA, bkB, bkC = b.banks["bk5"], b.banks["bk6"], b.banks["bk7"]
    mu_bc, rstd_bc = b.sb[pfx + "mub"], b.sb[pfx + "rsb"]
    ev_ln = emit_ln(b, x_buf, nt, mu_bc, rstd_bc, dep_start, bkA, bkB, pfx)
    xn = b.sb[pfx + "xn"]
    sc32 = b.sb[pfx + "sc32"]
    ev_xn = None
    for kt in range(16):
        xs = x_buf[:, kt * nt:(kt + 1) * nt]
        e1 = b.tt(sc32[:, 0:nt], xs, mu_bc[:, 0:nt], ALU.subtract,
                  deps=[ev_ln, ev_xn] + list(dep_start))
        ev_xn = b.tt(xn[:, kt * nt:(kt + 1) * nt], sc32[:, 0:nt],
                     rstd_bc[:, 0:nt], ALU.mult, deps=[e1])
    if DEBUG_APS.get("mu") is not None:
        b.dma("G", DEBUG_APS["mu"], mu_bc[:, 0:nt], deps=[ev_ln])
        b.dma("G", DEBUG_APS["rs"], rstd_bc[:, 0:nt], deps=[ev_ln])
        b.dma("G", DEBUG_APS["xn"], xn[:, :], deps=[ev_xn])
        DEBUG_APS.clear()
    wst = b.sb["wst"]
    wbf = b.sb["wbf"]
    r1 = w1t.rearrange("(kt p) c -> p kt c", p=128)
    r3 = w3t.rearrange("(kt p) c -> p kt c", p=128)
    sg, hid, s1t = b.sb[pfx + "sg"], b.sb[pfx + "hid"], b.sb[pfx + "s1t"]
    ev_s1_last = None
    prev_c = dep_start[-1] if dep_start else None
    prev_mm = dep_start[0] if dep_start else None
    prev_eh = None
    first_dh = list(dep_start)
    for a in range(64):
        c1 = c2 = None
        for rr, base in ((r1, 0), (r3, 2048)):
            for hh in range(2):
                dh = b.dma("Y", wst[:, :].rearrange("p (kt c) -> p kt c", c=128),
                           rr[:, 8 * hh:8 * (hh + 1), 128 * a:128 * (a + 1)],
                           deps=[prev_c] + first_dh)
                first_dh = []
                prev_c = b.copy(wbf[:, base + 1024 * hh:base + 1024 * (hh + 1)],
                                wst[:, :], deps=[dh, prev_mm])
        c1 = c2 = prev_c
        for q in range(len(qs)):
            t0, tw = qoff[q], qs[q]
            eg = el = None
            for kt in range(16):
                xs = xn[:, kt * nt + t0: kt * nt + t0 + tw]
                eg = b.mm(bkA, (slice(0, 128), slice(0, tw)),
                          wbf[:, 128 * kt:128 * (kt + 1)], xs,
                          start=(kt == 0), stop=(kt == 15),
                          deps=[c1, ev_xn] if kt == 0 else [eg])
            for kt in range(16):
                xs = xn[:, kt * nt + t0: kt * nt + t0 + tw]
                el = b.mm(bkB, (slice(0, 128), slice(0, tw)),
                          wbf[:, 2048 + 128 * kt:2048 + 128 * (kt + 1)], xs,
                          start=(kt == 0), stop=(kt == 15),
                          deps=[c2, ev_xn] if kt == 0 else [el])
            prev_mm = el
            es = b.rd(bkA, b.act(sg[:, 0:tw], bkA.t[:, 0:tw], AF.Silu,
                                 deps=[eg, prev_eh]))
            eh = b.rd(bkB, b.tt(hid[:, 0:tw], sg[:, 0:tw], bkB.t[:, 0:tw], ALU.mult,
                                deps=[es, el]))
            prev_eh = eh
            em = b.mm(bkC, (slice(0, 128), slice(0, tw)), b.sb["H128"][:, :],
                      hid[:, 0:tw], True, True, deps=[eh])
            ee = b.rd(bkC, b.copy(s1t[:, 0:tw], bkC.t[:, 0:tw],
                                  deps=[em, ev_s1_last]))
            ev_s1_last = b.dma("G", s1buf[128 * a:128 * (a + 1), t0:t0 + tw],
                               s1t[:, 0:tw], deps=[ee])
    l2t, fwt = b.sb[pfx + "l2t"], b.sb[pfx + "fwt"]
    ev_fw_last = None
    s1r = s1buf.rearrange("(bb a2) t -> a2 bb t", a2=64)
    prev_dl_cons = None
    for a2 in range(64):
        dl = b.dma("G", l2t[:, 0:nt], s1r[a2], deps=[ev_s1_last, prev_dl_cons])
        for q in range(len(qs)):
            t0, tw = qoff[q], qs[q]
            em = b.mm(bkC, (slice(0, 128), slice(0, tw)), b.sb["M2"][:, :],
                      l2t[:, t0:t0 + tw], True, True, deps=[dl])
            ee = b.rd(bkC, b.copy(fwt[:, 0:tw], bkC.t[:, 0:tw],
                                  deps=[em, ev_fw_last]))
            prev_dl_cons = em
            ev_fw_last = b.dma("G", fwbuf[128 * a2:128 * (a2 + 1), t0:t0 + tw],
                               fwt[:, 0:tw], deps=[ee])
    fst = b.sb[pfx + "fst0"]
    r2 = w2t.rearrange("(bb a2) d -> a2 bb d", a2=64)
    bklist = [b.banks["bk%d" % i] for i in range(8)]
    ev_out = None
    prev_wc = prev_c
    prev_fc = prev_mm
    for q in range(len(qs)):
        t0, tw = qoff[q], qs[q]
        for dgrp in range(2):
            mm_last = [None] * 8
            for a2 in range(64):
                dw = b.dma("Y", wst[:, 0:1024].rearrange("p (dt c) -> p dt c", c=128),
                           r2[a2, :, 1024 * dgrp:1024 * (dgrp + 1)].rearrange(
                               "bb (dt c) -> bb dt c", c=128), deps=[prev_wc])
                cw = b.copy(wbf[:, 0:1024], wst[:, 0:1024],
                            deps=[dw] + ([mm_last[7]] if a2 else [prev_fc]))
                prev_wc = cw
                df = b.dma("G", fst[:, 0:tw],
                           fwbuf[128 * a2:128 * (a2 + 1), t0:t0 + tw],
                           deps=[ev_fw_last, prev_fc])
                prev_fc = None
                for dd in range(8):
                    mm_last[dd] = b.mm(bklist[dd], (slice(0, 128), slice(0, tw)),
                                       wbf[:, 128 * dd:128 * (dd + 1)],
                                       fst[:, 0:tw],
                                       start=(a2 == 0), stop=(a2 == 63),
                                       deps=[cw, df] if a2 == 0 else [cw, df, mm_last[dd]])
                prev_fc = mm_last[7]
            for dd in range(8):
                kt = dgrp * 8 + dd
                xs = x_buf[:, kt * nt + t0: kt * nt + t0 + tw]
                ev_out = b.rd(bklist[dd],
                              b.stt(xs, bklist[dd].t[:, 0:tw], FFN_SCALE, xs,
                                    ALU.mult, ALU.add, deps=[mm_last[dd]]))
            prev_fc = ev_out
    return ev_out


def emit_gblock(b, gbuf, bbuf, n, bankG, deps, gt_dram):
    gb = b.sb["gb16"]
    Gf = b.sb["Gf"]
    Gam = b.sb["Gam"]
    GamB = b.sb["GamB"]
    GTf = b.sb["GTf"]
    gsl = gbuf[:, 32 * n:32 * (n + 1)]
    bsl = bbuf[:, 32 * n:32 * (n + 1)]
    e1 = deps[0] if deps else None
    e2 = b.mm(bankG, (slice(0, 128), slice(0, 32)), b.sb["UPONESF"][:, :], gsl,
              True, True, deps=list(deps))
    e3 = b.rd(bankG, b.copy(Gf[:, :], bankG.t[:, 0:32], deps=[e2]))
    e4 = b.act(Gam[:, :], Gf[:, :], AF.Exp, deps=[e3])
    e5 = b.tt(GamB[:, :], Gam[:, :], bsl, ALU.mult, deps=[e4])
    e6 = b.trans(bankG, (slice(0, 32), slice(128, 256)), Gf[:, :], b.sb["I128F"][:, :],
                 deps=[e3], f32=True)
    e7 = b.rd(bankG, b.copy(GTf[:, :], bankG.t[0:32, 128:256], deps=[e6]))
    GTrow = b.sb["GTrow"]
    e8 = b.dma("G", gt_dram, GTf[:, :], deps=[e7] + deps)
    e9 = b.dma("G", GTrow[0:1, :], gt_dram.rearrange("a bb -> (a bb)")[None, :],
               deps=[e8])
    return {"Gf": Gf, "Gam": Gam, "GamB": GamB, "GTrow": GTrow,
            "ev": [e4, e5, e9]}


def emit_headE1(b, g, h, bankE, hterm):
    e1 = b.mm(bankE, (slice(0, 128), slice(0, 128)), b.sb["ONES1F"][:, :],
              g["GTrow"][0:1, 128 * h:128 * (h + 1)], True, True,
              deps=list(hterm) + g["ev"])
    GCc = b.sb["GCc"]
    e2 = b.rd(bankE, b.copy(GCc[:, 0:1], bankE.t[:, 127:128], deps=[e1] + hterm))
    e3 = b.act(GCc[:, 1:2], GCc[:, 0:1], AF.Exp, deps=[e2] + hterm)
    e4 = b.tt(GCc[:, 3:4], GCc[:, 0:1], g["Gf"][:, h:h + 1], ALU.subtract, deps=[e2])
    e5 = b.act(GCc[:, 2:3], GCc[:, 3:4], AF.Exp, deps=[e4, e3])
    return {"E1": e1, "GC": GCc[:, 0:1], "GamC": GCc[:, 1:2], "Dc": GCc[:, 2:3],
            "ev": [e3, e5]}


def emit_solve(b, kf_s, bankK, bankR, bankB, g, hd, h, dep_k, e_nb, hterm):
    sc1 = b.sb["sc1"]
    dmA = b.sb["dmA"]
    Bn = b.sb["Bn"]
    Bt = b.sb["Bt"]
    B2n = b.sb["B2n"]
    B2t = b.sb["B2t"]
    ipb = b.sb["ipb"]
    Rb = b.sb["Rb"]
    ebank = b.banks["bkE"]
    eA = b.ts(sc1[:, 0:128], ebank.t[:, 0:128], g["Gf"][:, h:h + 1], 0.0,
              ALU.subtract, ALU.max, deps=[hd["E1"]] + hterm)
    b.rd(ebank, eA)
    eB = b.tt(sc1[:, 0:128], sc1[:, 0:128], b.sb["LMASK_S"][:, :], ALU.add, deps=[eA])
    eC = b.act(dmA[:, :], sc1[:, 0:128], AF.Exp, scale=-1.0, deps=[eB] + hterm)
    eD = b.rd(bankK, b.stt(Bn[:, :], bankK.t[:, 0:128],
                           b.sb["nbt_col"][:, 0:1], dmA[:, :], ALU.mult, ALU.mult,
                           deps=[dep_k, eC, e_nb]))
    eT = b.trans(b.banks["bkT"], (slice(0, 128), slice(0, 128)), Bn[:, :],
                 b.sb["I128B"][:, :], deps=[eD])
    eE = b.rd(b.banks["bkT"], b.copy(Bt[:, :], b.banks["bkT"].t16[0:128, 0:128], deps=[eT]))
    eF = b.tt(ipb[:, :], Bt[:, :], b.sb["I128B"][:, :], ALU.add, deps=[eE])
    eR = b.mm(bankR, (slice(0, 128), slice(0, 128)), b.sb["I128B"][:, :], ipb[:, :],
              True, False, deps=[eF])
    curN, curT = Bn, Bt
    evN, evT = eD, eE
    for k in range(1, 7):
        e_s1 = b.mm(bankB, (slice(0, 128), slice(0, 128)), curT[:, :], curN[:, :],
                    True, True, deps=[evN, evT])
        ev2n = b.rd(bankB, b.copy(B2n[:, :], bankB.t[:, 0:128], deps=[e_s1]))
        ev2t = None
        if k < 6:
            e_s2 = b.mm(bankB, (slice(0, 128), slice(128, 256)), curN[:, :], curT[:, :],
                        True, True, deps=[evN, evT, ev2n])
            ev2t = b.rd(bankB, b.copy(B2t[:, :], bankB.t[:, 128:256], deps=[e_s2]))
        eRb = b.rd(bankR, b.copy(Rb[:, :], bankR.t[:, 0:128], deps=[eR]))
        eR = b.mm(bankR, (slice(0, 128), slice(0, 128)), B2n[:, :], Rb[:, :],
                  False, k == 6, deps=[ev2n, eRb])
        curN, B2n = B2n, curN
        if k < 6:
            curT, B2t = B2t, curT
        evN, evT = ev2n, (ev2t if ev2t else evT)
    Rt = b.sb["Rt"]
    eRt = b.rd(bankR, b.copy(Rt[:, :], bankR.t[:, 0:128], deps=[eR]))
    return eRt


def build_launch_A(consts_np, debug=False):
    nc = bass.Bass("TRN2", target_bir_lowering=False)
    es = contextlib.ExitStack()
    P = Prog(nc)
    b = B(nc, P, es)
    NT = TEXT
    QS = [259, 256, 256, 256]
    QOFF = [0, 259, 515, 771]

    xT = nc.dram_tensor("xT", [2048, NT], F32, kind="ExternalInput").ap()
    W1T = nc.dram_tensor("W1T", [2048, 8192], F32, kind="ExternalInput").ap()
    W3T = nc.dram_tensor("W3T", [2048, 8192], F32, kind="ExternalInput").ap()
    W2T = nc.dram_tensor("W2T", [8192, 2048], F32, kind="ExternalInput").ap()
    WQ = nc.dram_tensor("WQ", [2048, 12288], F32, kind="ExternalInput").ap()
    WBA = nc.dram_tensor("WBA", [2048, 64], F32, kind="ExternalInput").ap()
    CW = nc.dram_tensor("CW", [128, 256], F32, kind="ExternalInput").ap()
    CB = nc.dram_tensor("CB", [128, 64], F32, kind="ExternalInput").ap()
    ALOGB = nc.dram_tensor("ALOGB", [128, 32], F32, kind="ExternalInput").ap()
    DTB = nc.dram_tensor("DTB", [128, 32], F32, kind="ExternalInput").ap()
    cds = {}
    for nm, shp, dt in CONST_SPECS:
        cds[nm] = nc.dram_tensor("c_" + nm, shp, dt, kind="ExternalInput").ap()
    o_x1 = nc.dram_tensor("o_x1", [2048, TOK], F32, kind="ExternalOutput").ap()
    o_qf = nc.dram_tensor("o_qf", [2048, TOK], BF16, kind="ExternalOutput").ap()
    o_kf = nc.dram_tensor("o_kf", [2048, TOK], BF16, kind="ExternalOutput").ap()
    o_kt = nc.dram_tensor("o_kt", [TOK, 2048], BF16, kind="ExternalOutput").ap()
    o_vt = nc.dram_tensor("o_vt", [TOK, 4096], BF16, kind="ExternalOutput").ap()
    o_z = nc.dram_tensor("o_z", [TOK, 4096], BF16, kind="ExternalOutput").ap()
    o_g = nc.dram_tensor("o_g", [128, 256], F32, kind="ExternalOutput").ap()
    o_bt = nc.dram_tensor("o_bt", [128, 256], F32, kind="ExternalOutput").ap()
    o_U = nc.dram_tensor("o_U", [NCH, 32, 128, 128], BF16, kind="ExternalOutput").ap()
    o_Wt = nc.dram_tensor("o_Wt", [NCH, 32, 128, 128], BF16, kind="ExternalOutput").ap()
    o_P = nc.dram_tensor("o_P", [32, 128, 128], F32, kind="ExternalOutput").ap()
    o_B = nc.dram_tensor("o_B", [32, 128, 128], F32, kind="ExternalOutput").ap()
    dbk = "ExternalOutput" if debug else "Internal"
    s1buf = nc.dram_tensor("s1buf", [8192, NT], BF16, kind=dbk).ap()
    fwbuf = nc.dram_tensor("fwbuf", [8192, NT], BF16, kind=dbk).ap()
    gt_dram = nc.dram_tensor("gt_dram", [32, 128], F32, kind="Internal").ap()
    states = nc.dram_tensor("states", [2, 32, 128, 128], F32, kind="Internal").ap()
    if debug:
        dbg_xn = nc.dram_tensor("dbg_xn", [128, 16 * NT], BF16,
                                kind="ExternalOutput").ap()
        dbg_mu = nc.dram_tensor("dbg_mu", [128, NT], F32, kind="ExternalOutput").ap()
        dbg_rs = nc.dram_tensor("dbg_rs", [128, NT], F32, kind="ExternalOutput").ap()

    with es:
        for i in range(8):
            b.mkbank("bk%d" % i)
        b.banks["bkT"] = b.banks["bk7"]
        b.banks["bkG"] = b.banks["bk0"]
        b.banks["bkE"] = b.banks["bk1"]
        b.banks["bkK"] = b.banks["bk2"]
        b.banks["bkR"] = b.banks["bk3"]
        b.banks["bkB"] = b.banks["bk4"]
        b.banks["bkA"] = b.banks["bk5"]
        b.banks["bkX"] = b.banks["bk6"]

        for nm, shp, dt in CONST_SPECS:
            b.sbuf(nm, shp, dt)
        x = b.sbuf("x", [128, 16 * NT], F32)
        b.sbuf("f_mub", [128, NT], F32)
        b.sbuf("f_rsb", [128, NT], F32)
        b.sbuf("f_xn", [128, 16 * NT], BF16)
        b.sbuf("f_sc32", [128, NT], F32)
        b.sbuf("f_xb", [128, NT], BF16)
        b.sbuf("f_sq", [128, NT], BF16)
        b.sbuf("f_st", [1, 3 * 1030], F32)
        b.sbuf("f_rb", [1, 2 * 1030], BF16)
        b.sbuf("wst", [128, 1024], F32)
        b.sbuf("wbf", [128, 4096], BF16)
        b.sbuf("f_sg", [128, 512], BF16)
        b.sbuf("f_hid", [128, 512], BF16)
        b.sbuf("f_s1t", [128, 512], BF16)
        b.sbuf("f_l2t", [128, NT], BF16)
        b.sbuf("f_fwt", [128, 512], BF16)
        b.sbuf("f_fst0", [128, 260], BF16)
        wt = b.sbuf("wt", [128, 260], BF16)
        halo = b.sbuf("halo", [128, 64 * 4], BF16)
        cvw = b.sbuf("cvw", [128, 64 * 4], F32)
        cvb = b.sbuf("cvb", [128, 64], F32)
        co = b.sbuf("co", [128, 256], BF16)
        sqt = b.sbuf("sqt", [128, 256], BF16)
        rowst = b.sbuf("rowst", [1, 2 * 256], F32)
        rowbt = b.sbuf("rowbt", [1, 256], BF16)
        tstage = b.sbuf("tstage", [128, 128], BF16)
        gbuf = b.sbuf("gbuf", [128, 256], F32)
        btbuf = b.sbuf("btbuf", [128, 256], F32)
        nbtbuf = b.sbuf("nbtbuf", [128, 256], F32)
        nalog = b.sbuf("nalog", [128, 32], F32)
        alogb = b.sbuf("alogb", [128, 32], F32)
        dtb = b.sbuf("dtb", [128, 32], F32)
        zt = b.sbuf("zt", [128, 512], BF16)
        for nm in ["sc1"]:
            b.sbuf(nm, [128, 128], F32)
        for nm in ["dmA", "Bn", "Bt", "B2n", "B2t", "ipb", "Rb", "Rt",
                   "kf_s", "kt_s", "vt_s", "xv", "xk", "U_bf", "Wt_bf",
                   "SBc", "ybf"]:
            b.sbuf(nm, [128, 128], BF16)
        b.sbuf("gb16", [128, 32], BF16)
        b.sbuf("Gf", [128, 32], F32)
        b.sbuf("Gam", [128, 32], F32)
        b.sbuf("GamB", [128, 32], F32)
        b.sbuf("GTf", [32, 128], F32)
        b.sbuf("GTrow", [1, 4096], F32)
        b.sbuf("GCc", [128, 4], F32)
        b.sbuf("nbt_col", [128, 1], F32)
        SBw = b.sbuf("SBw", [128, 128], F32)
        zf32 = b.sbuf("zf32", [128, 128], F32)

        ev_consts = []
        for nm, shp, dt in CONST_SPECS:
            ev_consts.append(b.dma("Y", b.sb[nm][:, :], cds[nm]))
        ev_x = b.dma("Y", x[:, :].rearrange("p (kt t) -> p kt t", t=NT),
                     xT.rearrange("(kt p) t -> p kt t", p=128))
        ev_cw = b.dma("Y", cvw[:, :], CW)
        ev_cb = b.dma("Y", cvb[:, :], CB)
        ev_al = b.dma("Y", alogb[:, :], ALOGB)
        ev_dt = b.dma("Y", dtb[:, :], DTB)
        ev_nal1 = b.act(nalog[:, :], alogb[:, :], AF.Exp, deps=[ev_al])
        ev_nal = b.ts1(nalog[:, :], nalog[:, :], -1.0, ALU.mult, deps=[ev_nal1])
        dep0 = ev_consts + [ev_x]

        if debug:
            DEBUG_APS.update({"mu": dbg_mu, "rs": dbg_rs, "xn": dbg_xn})
        ev_ffn = emit_ffn(b, x[:, :], NT, QS, QOFF, W1T, W3T, W2T,
                          s1buf, fwbuf, "f_", dep0)
        ev_x1out = b.dma("G", o_x1.rearrange("(kt p) t -> p kt t", p=128),
                         x[:, :].rearrange("p (kt t) -> p kt t", t=NT)[:, :, 3:NT],
                         deps=[ev_ffn])


        bkS, bkS2 = b.banks["bk5"], b.banks["bk6"]
        ev_ln2 = emit_ln(b, x[:, :], NT, b.sb["f_mub"], b.sb["f_rsb"],
                         [ev_ffn], bkS, bkS2, "f_")
        xn = b.sb["f_xn"]
        sc32 = b.sb["f_sc32"]
        ev_xn = None
        for kt in range(16):
            xs = x[:, kt * NT:(kt + 1) * NT]
            e1 = b.tt(sc32[:, 0:NT], xs, b.sb["f_mub"][:, 0:NT], ALU.subtract,
                      deps=[ev_ln2, ev_xn])
            ev_xn = b.tt(xn[:, kt * NT:(kt + 1) * NT], sc32[:, 0:NT],
                         b.sb["f_rsb"][:, 0:NT], ALU.mult, deps=[e1])

        bkQ = b.banks["bk5"]
        bkL = b.banks["bk6"]
        bkT_ = b.banks["bkT"]
        rq = WQ.rearrange("(kt p) c -> p kt c", p=128)
        wq_free = None
        prev_er4 = prev_eb1 = None
        ev_conv_chain = {}
        ev_halo0 = b.memset(halo[:, :], 0.0)
        wqbf = b.sb["wbf"]
        for f in range(64):
            cw2 = wq_free if wq_free else ev_ffn
            for hh in range(2):
                dw = b.dma("Y", b.sb["wst"][:, :].rearrange("p (kt c) -> p kt c", c=128),
                           rq[:, 8 * hh:8 * (hh + 1), 128 * f:128 * (f + 1)],
                           deps=[cw2])
                cw2 = b.copy(wqbf[:, 1024 * hh:1024 * (hh + 1)], b.sb["wst"][:, :],
                             deps=[dw, wq_free, ev_ffn])
            last_emm = None
            ev_conv_prev = ev_conv_chain[f] if f in ev_conv_chain else ev_halo0
            for q in range(4):
                t0, tw = QOFF[q], QS[q]
                emm = None
                for kt in range(16):
                    xs = xn[:, kt * NT + t0: kt * NT + t0 + tw]
                    emm = b.mm(bkQ, (slice(0, 128), slice(0, tw)),
                               wqbf[:, 128 * kt:128 * (kt + 1)], xs,
                               start=(kt == 0), stop=(kt == 15),
                               deps=[cw2, ev_xn] if kt == 0 else [emm])
                last_emm = emm
                if q == 0:
                    eh = b.copy(wt[:, 0:3], halo[:, 4 * f:4 * f + 3], deps=[ev_conv_prev])
                    ee = b.rd(bkQ, b.copy(wt[:, 3 - 3 + 3:259] if False else wt[:, 0:259],
                                          bkQ.t[:, 0:259], deps=[emm])) if False else None
                    ee = b.rd(bkQ, b.copy(wt[:, 0:259], bkQ.t[:, 0:259], deps=[emm, eh]))
                else:
                    eh = b.copy(wt[:, 0:3], wt[:, 256:259], deps=[ev_conv_prev])
                    ee = b.rd(bkQ, b.copy(wt[:, 3:259], bkQ.t[:, 0:256], deps=[emm, eh]))
                cw_col = lambda j: cvw[:, 4 * f + j:4 * f + j + 1]
                ecv = b.ts1(co[:, :], wt[:, 0:256], cw_col(0), ALU.mult,
                            deps=[ee, ev_cw])
                for j in range(1, 4):
                    ecv = b.stt(co[:, :], wt[:, j:j + 256], cw_col(j), co[:, :],
                                ALU.mult, ALU.add, deps=[ecv])
                esl = b.act(co[:, :], co[:, :], AF.Silu, bias=cvb[:, f:f + 1],
                            deps=[ecv, ev_cb])
                ev_conv_prev = esl
                own0 = 256 * q
                if f < 32:
                    esq = b.act(sqt[:, :], co[:, :], AF.Square, deps=[esl])
                    er1 = b.mm(bkL, (slice(0, 1), slice(0, 256)), b.sb["ONESCB"][:, :],
                               sqt[:, :], True, True, deps=[esq])
                    er2 = b.rd(bkL, b.act(rowst[:, 0:256], bkL.t[0:1, 0:256], AF.Ln,
                                          bias=b.sb["FC"][0:1, 1:2], deps=[er1]))
                    bias2 = (b.sb["FC"][0:1, 3:4] if f < 16 else 0.0)
                    er3 = b.act(rowst[:, 256:512], rowst[:, 0:256], AF.Exp,
                                bias=bias2, scale=-0.5, deps=[er2, prev_er4])
                    er4 = b.copy(rowbt[:, 0:256], rowst[:, 256:512],
                                 deps=[er3, prev_eb1])
                    eb1 = b.mm(bkL, (slice(0, 128), slice(0, 256)), b.sb["ONES1B"][:, :],
                               rowbt[:, 0:256], True, True, deps=[er4])
                    prev_er4, prev_eb1 = er4, eb1
                    enrm = b.rd(bkL, b.tt(co[:, :], co[:, :], bkL.t[:, 0:256], ALU.mult,
                                          deps=[eb1, esl]))
                    dst = o_qf if f < 16 else o_kf
                    frow = f if f < 16 else f - 16
                    ev_sv = b.dma("G", dst[128 * frow:128 * (frow + 1),
                                           own0:own0 + 256], co[:, :], deps=[enrm])
                    ev_conv_prev = ev_sv
                    if f >= 16:
                        for hh in range(2):
                            etr = b.trans(bkT_, (slice(0, 128), slice(0, 128)),
                                          co[:, 128 * hh:128 * (hh + 1)],
                                          b.sb["I128B"][:, :], deps=[enrm])
                            ecp = b.rd(bkT_, b.copy(tstage[:, :],
                                                    bkT_.t16[0:128, 0:128],
                                                    deps=[etr, ev_conv_prev]))
                            ev_sv = b.dma("G", o_kt[own0 + 128 * hh:own0 + 128 * (hh + 1),
                                                    128 * frow:128 * (frow + 1)],
                                          tstage[:, :], deps=[ecp])
                            ev_conv_prev = ev_sv
                else:
                    fv = f - 32
                    for hh in range(2):
                        etr = b.trans(bkT_, (slice(0, 128), slice(0, 128)),
                                      co[:, 128 * hh:128 * (hh + 1)],
                                      b.sb["I128B"][:, :], deps=[esl])
                        ecp = b.rd(bkT_, b.copy(tstage[:, :], bkT_.t16[0:128, 0:128],
                                                deps=[etr, ev_conv_prev]))
                        ev_sv = b.dma("G", o_vt[own0 + 128 * hh:own0 + 128 * (hh + 1),
                                                128 * fv:128 * (fv + 1)],
                                      tstage[:, :], deps=[ecp])
                        ev_conv_prev = ev_sv
            wq_free = last_emm
            ev_conv_chain[f] = ev_conv_prev
        ev_p3 = P.all_events()

        bkM = b.banks["bk5"]
        rba = WBA.rearrange("(kt p) c -> p kt c", p=128)
        emm_ba = None
        for kt in range(16):
            dws = b.dma("Y", b.sb["wst"][:, 0:64], rba[:, kt, :],
                        deps=[emm_ba] if emm_ba else ev_p3)
            cba = b.copy(b.sb["wbf"][:, 0:64], b.sb["wst"][:, 0:64],
                         deps=([dws, emm_ba] if emm_ba else [dws] + ev_p3))
            for m in range(8):
                lhs = xn[:, kt * NT + 3 + 128 * m: kt * NT + 3 + 128 * (m + 1)]
                emm_ba = b.mm(bkM, (slice(0, 128), slice(64 * m, 64 * (m + 1))),
                              lhs, b.sb["wbf"][:, 0:64],
                              start=(kt == 0 and m == 0),
                              stop=(kt == 15 and m == 7),
                              deps=[cba, ev_xn] if kt == 0 else [cba, emm_ba])
        ev_bg = []
        for m in range(8):
            braw = bkM.t[:, 64 * m:64 * m + 32]
            araw = bkM.t[:, 64 * m + 32:64 * m + 64]
            bsl = btbuf[:, 32 * m:32 * (m + 1)]
            nbsl = nbtbuf[:, 32 * m:32 * (m + 1)]
            gsl = gbuf[:, 32 * m:32 * (m + 1)]
            t1 = b.sb["Gf"]
            t2 = b.sb["Gam"]
            e1 = b.rd(bkM, b.act(bsl, braw, AF.Exp, scale=-1.0, deps=[emm_ba]))
            e2 = b.ts1(bsl, bsl, 1.0, ALU.add, deps=[e1])
            e3 = b.recip(bsl, bsl, deps=[e2])
            e4 = b.ts1(nbsl, bsl, -1.0, ALU.mult, deps=[e3])
            e5 = b.rd(bkM, b.tt(t1[:, :], araw, dtb[:, :], ALU.add, deps=[emm_ba, ev_dt]))
            e6 = b.act(t2[:, :], t1[:, :], AF.Abs, deps=[e5])
            e7 = b.act(t2[:, :], t2[:, :], AF.Exp, scale=-1.0, deps=[e6])
            e8 = b.act(t2[:, :], t2[:, :], AF.Ln, bias=b.sb["FC"][:, 2:3], deps=[e7])
            e9 = b.act(t1[:, :], t1[:, :], AF.Relu, deps=[e5, e8])
            e10 = b.tt(t1[:, :], t1[:, :], t2[:, :], ALU.add, deps=[e8, e9])
            e11 = b.tt(gsl, t1[:, :], nalog[:, :], ALU.mult, deps=[e10, ev_nal])
            ev_bg += [e4, e11]
        ev_gout = b.dma("G", o_g, gbuf[:, :], deps=ev_bg)
        ev_btout = b.dma("G", o_bt, btbuf[:, :], deps=ev_bg)

        zbanks = [b.banks["bk%d" % i] for i in range(8)]
        ev_p4 = P.all_events()
        zw_free = None
        ev_zprev = None
        for cb_ in range(8):
            mm_z = [None] * 8
            for kt in range(16):
                dz = b.dma("Y", b.sb["wst"][:, 0:512],
                           WQ.rearrange("(kt p) c -> p kt c", p=128)
                           [:, kt, 8192 + 512 * cb_:8192 + 512 * (cb_ + 1)],
                           deps=[zw_free] if zw_free else ev_p4)
                cz = b.copy(b.sb["wbf"][:, 0:512], b.sb["wst"][:, 0:512],
                            deps=([dz, zw_free] if zw_free else [dz] + ev_p4))
                zw_free = cz
                for m in range(8):
                    lhs = xn[:, kt * NT + 3 + 128 * m: kt * NT + 3 + 128 * (m + 1)]
                    mm_z[m] = b.mm(zbanks[m], (slice(0, 128), slice(0, 512)),
                                   lhs, b.sb["wbf"][:, 0:512],
                                   start=(kt == 0), stop=(kt == 15),
                                   deps=[cz, ev_xn] if kt == 0 else [cz, mm_z[m]])
                zw_free = mm_z[7]
            ev_zdma = None
            for m in range(8):
                ez = b.rd(zbanks[m], b.act(zt[:, :], zbanks[m].t[:, 0:512], AF.Silu,
                                           deps=[mm_z[m], ev_zprev]))
                ev_zdma = b.dma("G", o_z[128 * m:128 * (m + 1),
                                         512 * cb_:512 * (cb_ + 1)],
                                zt[:, :], deps=[ez])
                ev_zprev = ev_zdma

        bankG = b.banks["bkG"]
        bankE = b.banks["bkE"]
        bankK = b.banks["bkK"]
        bankR = b.banks["bkR"]
        bankB_ = b.banks["bkB"]
        bankA = b.banks["bkA"]
        bankX = b.banks["bkX"]
        kf_s, kt_s, vt_s = b.sb["kf_s"], b.sb["kt_s"], b.sb["vt_s"]
        xv, xk = b.sb["xv"], b.sb["xk"]
        U_bf, Wt_bf, SBc, ybf = b.sb["U_bf"], b.sb["Wt_bf"], b.sb["SBc"], b.sb["ybf"]
        ev_z = b.memset(zf32[:, :], 0.0)
        ev_SB = {}
        ev_SP = {}
        for h in range(32):
            ev_SB[h] = b.dma("G", states[0, h], zf32[:, :], deps=[ev_z])
            ev_SP[h] = b.dma("G", states[1, h], b.sb["I128F"][:, :],
                             deps=[ev_consts[3]])
        for n in range(NCH):
            hterm = P.all_events()
            g = emit_gblock(b, gbuf, btbuf, n, bankG, ev_bg[-2:] + hterm, gt_dram)
            own0 = 128 * n
            eKK = None
            ev_kt = None
            for h in range(32):
                kv = h // 2
                hterm = P.all_events()
                if h % 2 == 0:
                    d1 = b.dma("Y", kf_s[:, :], o_kf[128 * kv:128 * (kv + 1),
                                                     own0:own0 + 128], deps=hterm)
                    d2 = b.dma("Y", kt_s[:, :], o_kt[own0:own0 + 128,
                                                     128 * kv:128 * (kv + 1)], deps=hterm)
                    ev_kt = d2
                    eKK = b.mm(bankK, (slice(0, 128), slice(0, 128)), kf_s[:, :],
                               kf_s[:, :], True, True, deps=[d1])
                d3 = b.dma("Y", vt_s[:, :], o_vt[own0:own0 + 128,
                                                 128 * h:128 * (h + 1)], deps=hterm)
                hd = emit_headE1(b, g, h, bankE, hterm)
                e_nb = b.copy(b.sb["nbt_col"][:, :],
                              nbtbuf[:, 32 * n + h:32 * n + h + 1],
                              deps=[ev_bg[-2]] + hterm)
                eRt = emit_solve(b, kf_s, bankK, bankR, bankB_, g, hd, h, eKK,
                                 e_nb, hterm)
                e_xv = b.ts1(xv[:, :], vt_s[:, :], btbuf[:, 32 * n + h:32 * n + h + 1],
                             ALU.mult, deps=[d3] + hterm)
                e_xk = b.ts1(xk[:, :], kt_s[:, :], g["GamB"][:, h:h + 1],
                             ALU.mult, deps=[ev_kt] + g["ev"] + hterm)
                eU = b.mm(bankA, (slice(0, 128), slice(0, 128)), b.sb["Rt"][:, :],
                          xv[:, :], True, True, deps=[eRt, e_xv])
                eUc = b.rd(bankA, b.copy(U_bf[:, :], bankA.t[:, 0:128],
                                         deps=[eU] + hterm))
                b.dma("G", o_U[n, h], U_bf[:, :], deps=[eUc])
                eW = b.mm(bankA, (slice(0, 128), slice(128, 256)), xk[:, :],
                          b.sb["Rt"][:, :], True, True, deps=[eRt, e_xk])
                eWc = b.rd(bankA, b.copy(Wt_bf[:, :], bankA.t[:, 128:256],
                                         deps=[eW] + hterm))
                b.dma("G", o_Wt[n, h], Wt_bf[:, :], deps=[eWc])
                dSB = b.dma("G", SBw[:, :], states[0, h], deps=[ev_SB[h]] + hterm)
                ec1 = b.copy(SBc[:, :], SBw[:, :], deps=[dSB])
                eB1 = b.mm(bankA, (slice(0, 128), slice(256, 384)), Wt_bf[:, :],
                           SBc[:, :], True, True, deps=[eWc, ec1])
                eY = b.rd(bankA, b.tt(ybf[:, :], U_bf[:, :], bankA.t[:, 256:384],
                                      ALU.subtract, deps=[eB1, eUc] + hterm))
                eYd = b.ts1(ybf[:, :], ybf[:, :], hd["Dc"], ALU.mult,
                            deps=[eY] + hd["ev"])
                eS2 = b.mm(bankA, (slice(0, 128), slice(384, 512)), kt_s[:, :],
                           ybf[:, :], True, True, deps=[eYd, ev_kt])
                eSB = b.rd(bankA, b.stt(SBw[:, :], SBw[:, :], hd["GamC"],
                                        bankA.t[:, 384:512],
                                        ALU.mult, ALU.add, deps=[eS2, ec1]))
                ev_SB[h] = b.dma("G", states[0, h], SBw[:, :], deps=[eSB])
                dSP = b.dma("G", SBw[:, :], states[1, h], deps=[ev_SP[h], eSB,
                                                                ev_SB[h]])
                ec2 = b.copy(SBc[:, :], SBw[:, :], deps=[dSP, eB1])
                eP1 = b.mm(bankX, (slice(0, 128), slice(0, 128)), Wt_bf[:, :],
                           SBc[:, :], True, True, deps=[eWc, ec2])
                eYP = b.rd(bankX, b.ts1(ybf[:, :], bankX.t[:, 0:128], hd["Dc"],
                                        ALU.mult, deps=[eP1, eS2] + hd["ev"]))
                eP2 = b.mm(bankX, (slice(0, 128), slice(128, 256)), kt_s[:, :],
                           ybf[:, :], True, True, deps=[eYP, ev_kt])
                eSP = b.rd(bankX, b.stt(SBw[:, :], SBw[:, :], hd["GamC"],
                                        bankX.t[:, 128:256],
                                        ALU.mult, ALU.subtract, deps=[eP2, ec2]))
                ev_SP[h] = b.dma("G", states[1, h], SBw[:, :], deps=[eSP])
        evf = []
        for h in range(32):
            evf.append(b.dma("G", o_B[h], states[0, h], deps=[ev_SB[h]]))
            evf.append(b.dma("G", o_P[h], states[1, h], deps=[ev_SP[h]]))
        fin = P.all_events()
        P.op("G", lambda e: e.nop(), deps=fin)
        P.realize()
    return nc


def build_launch_B(consts_np, debug=False):
    nc = bass.Bass("TRN2", target_bir_lowering=False)
    es = contextlib.ExitStack()
    P = Prog(nc)
    b = B(nc, P, es)
    NT = TOK
    QS = [256, 256, 256, 256]
    QOFF = [0, 256, 512, 768]

    i_x1 = nc.dram_tensor("i_x1", [2048, TOK], F32, kind="ExternalInput").ap()
    i_qf = nc.dram_tensor("i_qf", [2048, TOK], BF16, kind="ExternalInput").ap()
    i_kf = nc.dram_tensor("i_kf", [2048, TOK], BF16, kind="ExternalInput").ap()
    i_kt = nc.dram_tensor("i_kt", [TOK, 2048], BF16, kind="ExternalInput").ap()
    i_vt = nc.dram_tensor("i_vt", [TOK, 4096], BF16, kind="ExternalInput").ap()
    i_z = nc.dram_tensor("i_z", [TOK, 4096], BF16, kind="ExternalInput").ap()
    i_g = nc.dram_tensor("i_g", [128, 256], F32, kind="ExternalInput").ap()
    i_bt = nc.dram_tensor("i_bt", [128, 256], F32, kind="ExternalInput").ap()
    i_U = nc.dram_tensor("i_U", [NCH, 32, 128, 128], BF16, kind="ExternalInput").ap()
    i_Wt = nc.dram_tensor("i_Wt", [NCH, 32, 128, 128], BF16, kind="ExternalInput").ap()
    i_PseqT = nc.dram_tensor("i_PseqT", [3, 32, 128, 128], F32, kind="ExternalInput").ap()
    i_Bseq = nc.dram_tensor("i_Bseq", [3, 32, 128, 128], F32, kind="ExternalInput").ap()
    W1T = nc.dram_tensor("W1T", [2048, 8192], F32, kind="ExternalInput").ap()
    W3T = nc.dram_tensor("W3T", [2048, 8192], F32, kind="ExternalInput").ap()
    W2T = nc.dram_tensor("W2T", [8192, 2048], F32, kind="ExternalInput").ap()
    WO = nc.dram_tensor("WO", [4096, 2048], F32, kind="ExternalInput").ap()
    NORMWB = nc.dram_tensor("NORMWB", [128, 128], F32, kind="ExternalInput").ap()
    cds = {}
    for nm, shp, dt in CONST_SPECS:
        cds[nm] = nc.dram_tensor("c_" + nm, shp, dt, kind="ExternalInput").ap()
    o_out = nc.dram_tensor("o_out", [2048, TOK], F32, kind="ExternalOutput").ap()
    oT_dram = nc.dram_tensor("oT_dram", [4096, TOK], BF16,
                             kind=("ExternalOutput" if debug else "Internal")).ap()
    if debug:
        dbg_x2 = nc.dram_tensor("dbg_x2", [128, 16 * TOK], F32,
                                kind="ExternalOutput").ap()
    if debug in (2, 4):
        dbg_O = nc.dram_tensor("dbg_O", [NCH, 32, 128, 128], F32,
                               kind="ExternalOutput").ap()
        dbg_Vp = nc.dram_tensor("dbg_Vp", [NCH, 32, 128, 128], BF16,
                                kind="ExternalOutput").ap()
        dbg_Mm = nc.dram_tensor("dbg_Mm", [NCH, 32, 128, 128], BF16,
                                kind="ExternalOutput").ap()
    s1buf = nc.dram_tensor("s1buf", [8192, NT], BF16, kind="Internal").ap()
    fwbuf = nc.dram_tensor("fwbuf", [8192, NT], BF16, kind="Internal").ap()
    gt_dram = nc.dram_tensor("gt_dram", [32, 128], F32, kind="Internal").ap()

    with es:
        for i in range(8):
            b.mkbank("bk%d" % i)
        b.banks["bkT"] = b.banks["bk7"]
        b.banks["bkG"] = b.banks["bk0"]
        b.banks["bkE"] = b.banks["bk1"]
        b.banks["bkK"] = b.banks["bk2"]
        b.banks["bkA"] = b.banks["bk3"]
        b.banks["bkX"] = b.banks["bk4"]

        for nm, shp, dt in CONST_SPECS:
            b.sbuf(nm, shp, dt)
        x2 = b.sbuf("x2", [128, 16 * NT], F32)
        b.sbuf("f_mub", [128, NT], F32)
        b.sbuf("f_rsb", [128, NT], F32)
        b.sbuf("f_xn", [128, 16 * NT], BF16)
        b.sbuf("f_sc32", [128, NT], F32)
        b.sbuf("f_xb", [128, NT], BF16)
        b.sbuf("f_sq", [128, NT], BF16)
        b.sbuf("f_st", [1, 3 * 1030], F32)
        b.sbuf("f_rb", [1, 2 * 1030], BF16)
        b.sbuf("wst", [128, 1024], F32)
        b.sbuf("wbf", [128, 4096], BF16)
        b.sbuf("f_sg", [128, 512], BF16)
        b.sbuf("f_hid", [128, 512], BF16)
        b.sbuf("f_s1t", [128, 512], BF16)
        b.sbuf("f_l2t", [128, NT], BF16)
        b.sbuf("f_fwt", [128, 512], BF16)
        b.sbuf("f_fst0", [128, 260], BF16)
        gbuf = b.sbuf("gbuf", [128, 256], F32)
        btbuf = b.sbuf("btbuf", [128, 256], F32)
        normw = b.sbuf("normw", [128, 128], F32)
        b.sbuf("gb16", [128, 32], BF16)
        b.sbuf("Gf", [128, 32], F32)
        b.sbuf("Gam", [128, 32], F32)
        b.sbuf("GamB", [128, 32], F32)
        b.sbuf("GTf", [32, 128], F32)
        b.sbuf("GTrow", [1, 4096], F32)
        b.sbuf("GCc", [128, 4], F32)
        for nm in ["sc1", "o1", "o2"]:
            b.sbuf(nm, [128, 128], F32)
        for nm in ["dmM", "kf_s", "qf_s", "kt_s", "vt_s", "z_s", "U_s", "Wt_s",
                   "mmT", "vp", "Sc", "ybf", "onrm", "tstage", "PT_bf"]:
            b.sbuf(nm, [128, 128], BF16)
        b.sbuf("rcol", [128, 2], F32)
        pbst = b.sbuf("pbst", [128, 128], F32)
        S = b.sbuf("S", [128, 32 * 128], F32)
        woL = b.sbuf("woL", [128, 32 * 128], BF16)

        ev_consts = []
        for nm, shp, dt in CONST_SPECS:
            ev_consts.append(b.dma("Y", b.sb[nm][:, :], cds[nm]))
        ev_x1 = b.dma("Y", x2[:, :].rearrange("p (kt t) -> p kt t", t=NT),
                      i_x1.rearrange("(kt p) t -> p kt t", p=128))
        ev_g_in = b.dma("Y", gbuf[:, :], i_g)
        ev_bt_in = b.dma("Y", btbuf[:, :], i_bt)
        ev_nw = b.dma("Y", normw[:, :], NORMWB)

        bankA = b.banks["bkA"]
        ev_S = {}
        prev = None
        for h in range(32):
            Sh = S[:, 128 * h:128 * (h + 1)]
            ev = b.memset(Sh, 0.0)
            for j in range(3):
                d1 = b.dma("Y", pbst[:, :], i_PseqT[j, h],
                           deps=[prev] if prev else [])
                c1 = b.copy(b.sb["PT_bf"][:, :], pbst[:, :], deps=[d1, prev])
                c2 = b.copy(b.sb["Sc"][:, :], Sh, deps=[ev, prev])
                em = b.mm(bankA, (slice(0, 128), slice(0, 128)), b.sb["PT_bf"][:, :],
                          b.sb["Sc"][:, :], True, True, deps=[c1, c2])
                d2 = b.dma("Y", pbst[:, :], i_Bseq[j, h], deps=[c1])
                ev = b.rd(bankA, b.tt(Sh, bankA.t[:, 0:128], pbst[:, :], ALU.add,
                                      deps=[em, d2]))
                prev = ev
            ev_S[h] = ev

        bankG, bankE, bankK, bankX = (b.banks["bkG"], b.banks["bkE"],
                                      b.banks["bkK"], b.banks["bkX"])
        bkT_ = b.banks["bkT"]
        kf_s, qf_s, kt_s, vt_s = (b.sb["kf_s"], b.sb["qf_s"], b.sb["kt_s"],
                                  b.sb["vt_s"])
        z_s, U_s, Wt_s = b.sb["z_s"], b.sb["U_s"], b.sb["Wt_s"]
        dmM, mmT, vp, Sc, ybf = (b.sb["dmM"], b.sb["mmT"], b.sb["vp"],
                                 b.sb["Sc"], b.sb["ybf"])
        o1, o2, onrm = b.sb["o1"], b.sb["o2"], b.sb["onrm"]
        rcol = b.sb["rcol"]
        ev_state = dict(ev_S)
        for n in range(NCH):
            hterm = P.all_events()
            g = emit_gblock(b, gbuf, btbuf, n, bankG, [ev_g_in, ev_bt_in] + hterm, gt_dram)
            own0 = 128 * n
            eKQ = None
            dkt = dqf = None
            for h in range(32):
                kv = h // 2
                hterm = P.all_events()
                if True:
                    dkf = b.dma("Y", kf_s[:, :], i_kf[128 * kv:128 * (kv + 1),
                                                      own0:own0 + 128], deps=hterm)
                    dqf = b.dma("Y", qf_s[:, :], i_qf[128 * kv:128 * (kv + 1),
                                                      own0:own0 + 128], deps=hterm)
                    dkt = b.dma("Y", kt_s[:, :], i_kt[own0:own0 + 128,
                                                      128 * kv:128 * (kv + 1)], deps=hterm)
                    eKQ = b.mm(bankK, (slice(0, 128), slice(0, 128)), kf_s[:, :],
                               qf_s[:, :], True, True, deps=[dkf, dqf])
                dvt = b.dma("Y", vt_s[:, :], i_vt[own0:own0 + 128,
                                                  128 * h:128 * (h + 1)], deps=hterm)
                dz = b.dma("Y", z_s[:, :], i_z[own0:own0 + 128,
                                               128 * h:128 * (h + 1)], deps=hterm)
                dU = b.dma("G", U_s[:, :], i_U[n, h], deps=hterm)
                dW = b.dma("G", Wt_s[:, :], i_Wt[n, h], deps=hterm)
                hd = emit_headE1(b, g, h, bankE, hterm)
                eA = b.rd(bankE, b.ts(b.sb["sc1"][:, 0:128], bankE.t[:, 0:128],
                                      g["Gf"][:, h:h + 1], 0.0, ALU.subtract, ALU.min,
                                      deps=[hd["E1"]] + hterm))
                eB2 = b.tt(b.sb["sc1"][:, 0:128], b.sb["sc1"][:, 0:128],
                           b.sb["LMASK_IT"][:, :], ALU.add, deps=[eA])
                eC = b.act(dmM[:, :], b.sb["sc1"][:, 0:128], AF.Exp,
                           deps=[eB2] + hterm)
                eM = b.rd(bankK, b.tt(mmT[:, :], bankK.t[:, 0:128], dmM[:, :], ALU.mult,
                                      deps=[eKQ, eC] + hterm))
                eSc = b.copy(Sc[:, :], S[:, 128 * h:128 * (h + 1)],
                             deps=[ev_state[h]] + hterm)
                bkW, bkQ, bkO, bkS = bankA, bankX, b.banks["bk5"], b.banks["bk6"]
                eW1 = b.mm(bkW, (slice(0, 128), slice(0, 128)), Wt_s[:, :], Sc[:, :],
                           True, True, deps=[dW, eSc])
                eVp = b.rd(bkW, b.tt(vp[:, :], U_s[:, :], bkW.t[:, 0:128],
                                     ALU.subtract, deps=[eW1, dU]))
                eQ1 = b.mm(bkQ, (slice(0, 128), slice(0, 128)), qf_s[:, :], Sc[:, :],
                           True, True, deps=[dqf, eSc])
                eOt = b.rd(bkQ, b.ts1(o1[:, :], bkQ.t[:, 0:128],
                                      g["Gam"][:, h:h + 1], ALU.mult,
                                      deps=[eQ1] + g["ev"]))
                eO2 = b.mm(bkO, (slice(0, 128), slice(0, 128)), mmT[:, :], vp[:, :],
                           True, True, deps=[eM, eVp])
                eO = b.rd(bkO, b.tt(o1[:, :], o1[:, :], bkO.t[:, 0:128], ALU.add,
                                    deps=[eO2, eOt]))
                eYd = b.ts1(ybf[:, :], vp[:, :], hd["Dc"], ALU.mult,
                            deps=[eVp] + hd["ev"])
                eS2 = b.mm(bkS, (slice(0, 128), slice(0, 128)), kt_s[:, :],
                           ybf[:, :], True, True, deps=[eYd, dkt])
                eSu = b.rd(bkS, b.stt(S[:, 128 * h:128 * (h + 1)],
                                      S[:, 128 * h:128 * (h + 1)], hd["GamC"],
                                      bkS.t[:, 0:128], ALU.mult, ALU.add,
                                      deps=[eS2, eSc]))
                ev_state[h] = eSu
                if debug == 2:
                    b.dma("G", dbg_O[n, h], o1[:, :], deps=[eO])
                    b.dma("G", dbg_Vp[n, h], vp[:, :], deps=[eVp])
                    b.dma("G", dbg_Mm[n, h], mmT[:, :], deps=[eM])
                eg1 = b.tt(o2[:, :], o1[:, :], z_s[:, :], ALU.mult, deps=[eO, dz])
                eg2 = b.act(b.sb["sc1"][:, 0:128], o2[:, :], AF.Square, deps=[eg1])
                eg3 = b.reduce(rcol[:, 0:1], b.sb["sc1"][:, 0:128], ALU.add, deps=[eg2])
                eg4 = b.act(rcol[:, 1:2], rcol[:, 0:1], AF.Ln,
                            bias=b.sb["FC"][:, 1:2],
                            scale=1.0 / 128.0, deps=[eg3])
                eg5 = b.act(rcol[:, 1:2], rcol[:, 1:2], AF.Exp, scale=-0.5, deps=[eg4])
                eg6 = b.ts1(o2[:, :], o2[:, :], rcol[:, 1:2], ALU.mult,
                            deps=[eg5, eg1])
                eg7 = b.tt(onrm[:, :], o2[:, :], normw[:, :], ALU.mult,
                           deps=[eg6, ev_nw])
                etr = b.trans(bkT_, (slice(0, 128), slice(0, 128)), onrm[:, :],
                              b.sb["I128B"][:, :], deps=[eg7])
                ecp = b.rd(bkT_, b.copy(b.sb["tstage"][:, :], bkT_.t16[0:128, 0:128],
                                        deps=[etr] + hterm))
                b.dma("G", oT_dram[128 * h:128 * (h + 1), own0:own0 + 128],
                      b.sb["tstage"][:, :], deps=[ecp])
        ev_oT = P.all_events()

        bklist = [b.banks["bk%d" % i] for i in range(8)]
        prev_L = None
        prev_wc = ev_oT[-1]
        for q in range(8):
            t0 = 128 * q
            dL = b.dma("Y", woL[:, :].rearrange("p (of t) -> p of t", t=128),
                       oT_dram.rearrange("(of p) t -> p of t", p=128)[:, :, t0:t0 + 128],
                       deps=ev_oT + ([prev_L] if prev_L else []))
            for dgrp in range(2):
                mm_last = [None] * 8
                for of in range(32):
                    dw = b.dma("Y", b.sb["wst"][:, 0:1024].rearrange(
                                   "p (dt c) -> p dt c", c=128),
                               WO.rearrange("(of p) d -> p of d", p=128)
                               [:, of, 1024 * dgrp:1024 * (dgrp + 1)].rearrange(
                                   "p (dt c) -> p dt c", c=128),
                               deps=[prev_wc])
                    cw = b.copy(b.sb["wbf"][:, 0:1024], b.sb["wst"][:, 0:1024],
                                deps=[dw, prev_wc])
                    prev_wc = cw
                    for dd in range(8):
                        mm_last[dd] = b.mm(bklist[dd], (slice(0, 128), slice(0, 128)),
                                           b.sb["wbf"][:, 128 * dd:128 * (dd + 1)],
                                           woL[:, 128 * of:128 * (of + 1)],
                                           start=(of == 0), stop=(of == 31),
                                           deps=[cw, dL] if of == 0 else [cw, mm_last[dd]])
                    prev_wc = mm_last[7]
                for dd in range(8):
                    kt = dgrp * 8 + dd
                    xs = x2[:, kt * NT + t0: kt * NT + t0 + 128]
                    ee = b.rd(bklist[dd], b.tt(xs, xs, bklist[dd].t[:, 0:128], ALU.add,
                                               deps=[mm_last[dd], ev_x1]))
                prev_L = mm_last[7]
        ev_x2 = P.all_events()
        if debug:
            b.dma("G", dbg_x2, x2[:, :], deps=ev_x2)
        ev_ffn2 = emit_ffn(b, x2[:, :], NT, QS, QOFF, W1T, W3T, W2T,
                           s1buf, fwbuf, "f_", ev_x2)
        ev_out = b.dma("G", o_out.rearrange("(kt p) t -> p kt t", p=128),
                       x2[:, :].rearrange("p (kt t) -> p kt t", t=NT),
                       deps=[ev_ffn2])
        fin = P.all_events()
        P.op("G", lambda e: e.nop(), deps=fin)
        P.realize()
    return nc


def kernel(**inputs):
    consts = host_consts()
    W1T = np.ascontiguousarray(np.asarray(inputs["L0_ffn_W1"]).T)
    W3T = np.ascontiguousarray(np.asarray(inputs["L0_ffn_W3"]).T)
    W2T = np.ascontiguousarray(np.asarray(inputs["L0_ffn_W2"]).T)
    wq = np.asarray(inputs["L0_attn_w_qkvz"])
    qc = (np.arange(16)[:, None] * 768 + np.arange(128)[None, :]).reshape(-1)
    kc = qc + 128
    vc = (np.arange(16)[:, None] * 768 + 256 + np.arange(256)[None, :]).reshape(-1)
    zc = (np.arange(16)[:, None] * 768 + 512 + np.arange(256)[None, :]).reshape(-1)
    WQ = np.ascontiguousarray(wq[:, np.r_[qc, kc, vc, zc]])
    wba = np.asarray(inputs["L0_attn_w_ba"])
    bc_ = (np.arange(16)[:, None] * 4 + np.arange(2)[None, :]).reshape(-1)
    WBA = np.ascontiguousarray(wba[:, np.r_[bc_, bc_ + 2]])
    CW = np.ascontiguousarray(np.asarray(inputs["L0_attn_conv_w"], np.float32)
                              .reshape(64, 128, 4).transpose(1, 0, 2).reshape(128, 256))
    CB = np.ascontiguousarray(np.asarray(inputs["L0_attn_conv_b"], np.float32)
                              .reshape(64, 128).T)
    ALOGB = np.ascontiguousarray(np.broadcast_to(
        np.asarray(inputs["L0_attn_a_log"], np.float32)[None, :], (128, 32)))
    DTB = np.ascontiguousarray(np.broadcast_to(
        np.asarray(inputs["L0_attn_dt_bias"], np.float32)[None, :], (128, 32)))
    NORMWB = np.ascontiguousarray(np.broadcast_to(
        np.asarray(inputs["L0_attn_norm_w"], np.float32)[None, :], (128, 128)))
    WO = np.ascontiguousarray(np.asarray(inputs["L0_attn_w_o"], np.float32))
    x = np.asarray(inputs["L0_attn_x"], np.float32)

    cmaps = {"c_" + nm: consts[nm] for nm, _, _ in CONST_SPECS}
    in_A = []
    for c in range(NCORES):
        bid, slab = c // 4, c % 4
        s0 = slab * TOK
        xe = np.zeros((TEXT, 2048), np.float32)
        xe[3:] = x[bid, s0:s0 + TOK]
        if s0 > 0:
            xe[0:3] = x[bid, s0 - 3:s0]
        m = {"xT": np.ascontiguousarray(xe.T), "W1T": W1T, "W3T": W3T,
             "W2T": W2T, "WQ": WQ, "WBA": WBA, "CW": CW, "CB": CB,
             "ALOGB": ALOGB, "DTB": DTB}
        m.update(cmaps)
        in_A.append(m)

    ncA = build_launch_A(consts)
    resA = run_bass_kernel_spmd(ncA, in_A, core_ids=list(range(NCORES))).results

    in_B = []
    ident = np.broadcast_to(np.eye(128, dtype=np.float32)[None], (32, 128, 128))
    zero = np.zeros((32, 128, 128), np.float32)
    for c in range(NCORES):
        bid, slab = c // 4, c % 4
        Pseq = []
        Bseq = []
        for j in range(3):
            src = slab - 3 + j
            if src < 0:
                Pseq.append(ident)
                Bseq.append(zero)
            else:
                Pseq.append(np.asarray(resA[bid * 4 + src]["o_P"]).transpose(0, 2, 1))
                Bseq.append(np.asarray(resA[bid * 4 + src]["o_B"]))
        m = {"i_x1": np.asarray(resA[c]["o_x1"]),
             "i_qf": np.asarray(resA[c]["o_qf"]),
             "i_kf": np.asarray(resA[c]["o_kf"]),
             "i_kt": np.asarray(resA[c]["o_kt"]),
             "i_vt": np.asarray(resA[c]["o_vt"]),
             "i_z": np.asarray(resA[c]["o_z"]),
             "i_g": np.asarray(resA[c]["o_g"]),
             "i_bt": np.asarray(resA[c]["o_bt"]),
             "i_U": np.asarray(resA[c]["o_U"]),
             "i_Wt": np.asarray(resA[c]["o_Wt"]),
             "i_PseqT": np.ascontiguousarray(np.stack(Pseq)),
             "i_Bseq": np.ascontiguousarray(np.stack(Bseq)),
             "W1T": W1T, "W3T": W3T, "W2T": W2T, "WO": WO, "NORMWB": NORMWB}
        m.update(cmaps)
        in_B.append(m)

    ncB = build_launch_B(consts)
    resB = run_bass_kernel_spmd(ncB, in_B, core_ids=list(range(NCORES))).results

    out = np.zeros((2, 4096, 2048), np.float32)
    for c in range(NCORES):
        bid, slab = c // 4, c % 4
        out[bid, slab * TOK:(slab + 1) * TOK] = np.asarray(resB[c]["o_out"]).T
    return out
